# version 01 (final)
# baseline (speedup 1.0000x reference)
import math
from contextlib import ExitStack

import numpy as np
import ml_dtypes

import concourse.bass as bass
import concourse.mybir as mybir
from concourse.bass_utils import run_bass_kernel_spmd

F32 = mybir.dt.float32
BF16 = mybir.dt.bfloat16
AF = mybir.ActivationFunctionType
ALU = mybir.AluOpType
AX = mybir.AxisListType

BF = ml_dtypes.bfloat16

N_CORE = 8
B_CORE = 2
T = 2048
C = 1024
CC = C // 128
TT = T // 128
TG = T // 512
NEG = -1e9
EPS = 1e-5

ALL_BLOCKS = ("l0attn", "l0ffn", "l1attn", "l1ffn")
DBG = None


def _dbg(name):
    return DBG is not None and name in str(DBG).split(",")


ENGINES = ("pe", "act", "dve", "pool", "sp")
SAFE_SAME = {"pe"}


DMA_SLOTS = 8


class Prog:
    def __init__(self, nc):
        self.nc = nc
        self.ops = []
        self.res = {}
        self.dma_uses = {}
        self.dma_rr = {e: 0 for e in ENGINES}

    def op(self, engine, fn, reads=(), writes=(), dma=False, tag=""):
        deps = set()
        for r in reads:
            st = self.res.get(r)
            if st and st["w"] is not None:
                deps.add(st["w"])
        for w in writes:
            st = self.res.get(w)
            if st:
                if st["w"] is not None:
                    deps.add(st["w"])
                deps.update(st["r"])
        oid = len(self.ops)
        slot = None
        dma_after = None
        if dma:
            slot = self.dma_rr[engine] % DMA_SLOTS
            self.dma_rr[engine] += 1
            key = (engine, slot)
            self.dma_uses[key] = self.dma_uses.get(key, 0) + 1
            dma_after = 16 * self.dma_uses[key]
        rec = {
            "id": oid, "eng": engine, "fn": fn, "deps": deps, "dma": dma,
            "slot": slot, "dma_after": dma_after, "inc": dma, "inc_val": None,
            "tag": tag, "barrier": False, "rd": tuple(reads), "wr": tuple(writes),
        }
        self.ops.append(rec)
        for r in reads:
            self.res.setdefault(r, {"w": None, "r": []})["r"].append(oid)
        for w in writes:
            self.res[w] = {"w": oid, "r": []}
        return oid

    def barrier(self):
        oid = len(self.ops)
        self.ops.append({
            "id": oid, "eng": None, "fn": None, "deps": set(), "dma": False,
            "dma_after": None, "inc": False, "inc_val": None, "tag": "barrier",
            "barrier": True,
        })
        self.res = {}
        return oid

    def finalize(self):
        ops = self.ops
        for rec in ops:
            if rec["barrier"]:
                continue
            for d in rec["deps"]:
                a = ops[d]
                if a["dma"]:
                    continue
                if a["eng"] != rec["eng"] or a["eng"] not in SAFE_SAME:
                    a["inc"] = True
        last_per_eng = {e: None for e in ENGINES}
        for rec in ops:
            if rec["barrier"]:
                for e in ENGINES:
                    lp = last_per_eng[e]
                    if lp is not None and not ops[lp]["dma"]:
                        ops[lp]["inc"] = True
                continue
            last_per_eng[rec["eng"]] = rec["id"]
        cnt = {e: 0 for e in ENGINES}
        for rec in ops:
            if rec["barrier"]:
                continue
            if rec["inc"] and not rec["dma"]:
                cnt[rec["eng"]] += 1
                rec["inc_val"] = cnt[rec["eng"]]
        sat = {e: {} for e in ENGINES}
        eng_prog = {e: 0 for e in ENGINES}
        dma_prog = {}
        for rec in ops:
            if rec["barrier"]:
                rec["bar_waits"] = {}
                for e in ENGINES:
                    ws = []
                    for o in ENGINES:
                        if eng_prog[o] > sat[e].get(("c", o), 0):
                            ws.append((("c", o), eng_prog[o]))
                    for (o, s), v in dma_prog.items():
                        if v > sat[e].get(("d", o, s), 0):
                            ws.append((("d", o, s), v))
                    for k, v in ws:
                        sat[e][k] = v
                    rec["bar_waits"][e] = ws
                continue
            e = rec["eng"]
            waits = []
            if rec["dma"] and rec["dma_after"] > 16:
                key, val = ("d", e, rec["slot"]), rec["dma_after"] - 16
                if sat[e].get(key, 0) < val:
                    waits.append((key, val))
                    sat[e][key] = val
            for d in sorted(rec["deps"]):
                a = ops[d]
                if a["dma"]:
                    key, val = ("d", a["eng"], a["slot"]), a["dma_after"]
                else:
                    if a["eng"] == e and a["eng"] in SAFE_SAME:
                        continue
                    if a["inc_val"] is None:
                        continue
                    key, val = ("c", a["eng"]), a["inc_val"]
                if sat[e].get(key, 0) >= val:
                    continue
                waits.append((key, val))
                sat[e][key] = val
            rec["waits"] = waits
            if rec["inc"] and not rec["dma"]:
                eng_prog[e] = rec["inc_val"]
            if rec["dma"]:
                dma_prog[("d?", 0)] if False else None
                dma_prog[(e, rec["slot"])] = rec["dma_after"]

    def emit(self, block, sems):
        ops = self.ops

        def run_engine(eng_name, eng):
            for rec in ops:
                if rec["barrier"]:
                    for key, val in rec["bar_waits"].get(eng_name, ()):
                        eng.wait_ge(sems[key], val)
                    continue
                if rec["eng"] != eng_name:
                    continue
                for key, val in rec["waits"]:
                    eng.wait_ge(sems[key], val)
                inst = rec["fn"](eng)
                if rec["dma"]:
                    inst.then_inc(sems[("d", eng_name, rec["slot"])], 16)
                elif rec["inc_val"] is not None:
                    inst.then_inc(sems[("c", eng_name)], 1)

        @block.tensor
        def _(t):
            run_engine("pe", t)

        @block.scalar
        def _(sc):
            run_engine("act", sc)

        @block.vector
        def _(v):
            run_engine("dve", v)

        @block.gpsimd
        def _(g):
            run_engine("pool", g)

        @block.sync
        def _(s):
            run_engine("sp", s)



def host_constants():
    c = {}
    c["c_ident_f32"] = np.eye(128, dtype=np.float32)
    c["c_ident_bf"] = np.eye(128, dtype=BF)
    r = np.arange(128)
    c["c_tri_causal"] = np.where(r[None, :] <= r[:, None], 0.0, NEG).astype(np.float32)
    c["c_tri_anti"] = np.where(r[None, :] >= r[:, None], 0.0, NEG).astype(np.float32)
    c["c_ones1"] = np.ones((1, 128), dtype=np.float32)
    c["c_ones_col"] = np.ones((128, 1), dtype=np.float32)
    c["c_eps"] = np.full((128, 1), EPS, dtype=np.float32)
    c["c_ones128_bf"] = np.ones((128, 128), dtype=BF)
    hsel = np.zeros((128, 2), dtype=np.float32)
    hsel[:64, 0] = 1.0
    hsel[64:, 1] = 1.0
    c["c_hsel"] = hsel
    c["c_hselT"] = np.ascontiguousarray(hsel.T)
    rm = np.zeros((128, 128), dtype=np.float32)
    for m in range(128):
        h0 = (m // 64) * 64
        d = m % 64
        if d < 32:
            rm[h0 + d + 32, m] = -1.0
        else:
            rm[h0 + d - 32, m] = 1.0
    c["c_rmat"] = rm
    DIM_HEAD, WINDOW, n = 64, 512, T
    half = np.arange(0, DIM_HEAD, 2, dtype=np.float64)
    inv_freq = 1.0 / 10000.0 ** (half / DIM_HEAD)
    t = np.arange(n, dtype=np.float64)
    freqs = t[:, None] * inv_freq[None, :]
    freqs = np.concatenate([freqs, freqs], axis=-1)
    base = (half + 0.4 * DIM_HEAD) / (1.4 * DIM_HEAD)
    power = (t - n // 2) / float(WINDOW // 2)
    scale = base[None, :] ** power[:, None]
    scale = np.concatenate([scale, scale], axis=-1)
    cos, sin = np.cos(freqs), np.sin(freqs)
    for name, tab in (
        ("c_rot_cosq", cos * scale), ("c_rot_sinq", sin * scale),
        ("c_rot_cosk", cos / scale), ("c_rot_sink", sin / scale),
    ):
        tabT = np.ascontiguousarray(tab.T.astype(np.float32))
        c[name] = np.concatenate([tabT, tabT], axis=0)
    return c


GLOBAL_CONSTS = [
    "c_ident_f32", "c_ident_bf", "c_tri_causal",
    "c_ones_col", "c_ones128_bf", "c_hsel", "c_hselT", "c_eps",
]
SCOPED_CONSTS = ["c_tri_anti", "c_ones1"]
L1F_CONSTS = ["c_rmat", "c_rot_cosq", "c_rot_sinq", "c_rot_cosk", "c_rot_sink"]

PM_SHAPES = {
    "pm_l0_b1": [128, 128], "pm_l0_b2": [128, 32], "pm_l0_bg": [128, 16],
    "pm_l1a_latb": [128, 3], "pm_l1a_wdb": [128, 48], "pm_l1a_outb": [128, 8],
    "pm_l1f_qs": [128, 2], "pm_l1f_ks": [128, 2],
    "pm_l1f_lng": [128, 16], "pm_l1f_lnb": [128, 16],
    "pm_l1f_b1a": [128, 44], "pm_l1f_b1g": [128, 44], "pm_l1f_b2": [128, 16],
}


def host_stage(inputs):
    s = {}

    def pm(vec, pad_to=None):
        v = np.asarray(vec, dtype=np.float32).reshape(-1)
        if pad_to is not None and v.shape[0] < pad_to:
            v = np.concatenate([v, np.zeros(pad_to - v.shape[0], np.float32)])
        n = v.shape[0]
        if n <= 128:
            col = np.zeros((128, 1), np.float32)
            col[:n, 0] = v
            return col
        assert n % 128 == 0
        return np.ascontiguousarray(v.reshape(-1, 128).T)

    s["pm_l0_b1"] = pm(inputs["L0_ffn_b1"])
    s["pm_l0_b2"] = pm(inputs["L0_ffn_b2"])
    s["pm_l0_bg"] = np.ascontiguousarray(
        np.broadcast_to(np.asarray(inputs["L0_ffn_bg"], np.float32)[:, None, :],
                        (4, 128, 4)).transpose(1, 0, 2).reshape(128, 16))
    s["pm_l1a_latb"] = pm(inputs["L1_attn_latent_b"], pad_to=384)
    s["pm_l1a_wdb"] = np.concatenate(
        [pm(inputs["L1_attn_Wd_b"][h], pad_to=384) for h in range(16)], axis=1)
    s["pm_l1a_outb"] = pm(inputs["L1_attn_out_b"])
    qs = np.asarray(inputs["L1_ffn_q_scale"], np.float32)
    ks = np.asarray(inputs["L1_ffn_k_scale"], np.float32)
    s["pm_l1f_qs"] = np.ascontiguousarray(np.tile(qs, (1, 2)).T)
    s["pm_l1f_ks"] = np.ascontiguousarray(np.tile(ks, (1, 2)).T)
    s["pm_l1f_lng"] = np.concatenate(
        [pm(inputs["L1_ffn_ln_g"][l]) for l in range(2)], axis=1)
    s["pm_l1f_lnb"] = np.concatenate(
        [pm(inputs["L1_ffn_ln_b"][l]) for l in range(2)], axis=1)
    HID = 2730
    NAC = 22
    b1 = np.asarray(inputs["L1_ffn_b1"], np.float32)
    s["pm_l1f_b1a"] = np.concatenate(
        [pm(b1[l][:HID], pad_to=NAC * 128) for l in range(2)], axis=1)
    s["pm_l1f_b1g"] = np.concatenate(
        [pm(b1[l][HID:], pad_to=NAC * 128) for l in range(2)], axis=1)
    s["pm_l1f_b2"] = np.concatenate(
        [pm(inputs["L1_ffn_b2"][l]) for l in range(2)], axis=1)
    return s


PASS_THROUGH = [
    "L0_attn_W_dq", "L0_attn_W_uq", "L0_attn_W_dkv", "L0_attn_W_uk",
    "L0_attn_W_uv", "L0_attn_W_o",
    "L0_ffn_Wg", "L0_ffn_W1", "L0_ffn_W2", "L0_ffn_b2",
    "L1_attn_latent_w", "L1_attn_Wd_w", "L1_attn_out_w",
    "L1_ffn_wqkv", "L1_ffn_wo", "L1_ffn_w1", "L1_ffn_w2",
]

W_SHAPES = {
    "L0_attn_W_dq": [512, C], "L0_attn_W_uq": [C, 512],
    "L0_attn_W_dkv": [512, C], "L0_attn_W_uk": [C, 512],
    "L0_attn_W_uv": [C, 512], "L0_attn_W_o": [C, C],
    "L0_ffn_Wg": [4, C, 4], "L0_ffn_W1": [4, C, 4096], "L0_ffn_W2": [4, 4096, C],
    "L0_ffn_b2": [4, C],
    "L1_attn_latent_w": [C, 288], "L1_attn_Wd_w": [16, C, 288],
    "L1_attn_out_w": [4608, C],
    "L1_ffn_wqkv": [2, C, 1536], "L1_ffn_wo": [2, 512, C],
    "L1_ffn_w1": [2, C, 5460], "L1_ffn_w2": [2, 2730, C],
}



class Ctx:
    def __init__(self, nc, prog):
        self.nc = nc
        self.p = prog
        self.d = {}
        self.t = {}

    def dram(self, name, shape, dtype, kind):
        ap = self.nc.dram_tensor(name, list(shape), dtype, kind=kind).ap()
        self.d[name] = ap
        return ap

    def sb(self, es, name, shape, dtype):
        h = es.enter_context(self.nc.sbuf_tensor(name, list(shape), dtype))
        self.t[name] = h
        return h

    def ps(self, es, name, shape, dtype=F32):
        h = es.enter_context(self.nc.psum_tensor(name, list(shape), dtype))
        self.t[name] = h
        return h

    def dma(self, queue, out_ap, in_ap, reads, writes, tag=""):
        self.p.op(queue, lambda e, o=out_ap, i=in_ap: e.dma_start(out=o, in_=i),
                  reads=reads, writes=writes, dma=True, tag=tag)

    def mm(self, out_ap, lhsT_ap, rhs_ap, start, stop, reads, writes, tag=""):
        self.p.op("pe",
                  lambda e, o=out_ap, l=lhsT_ap, r=rhs_ap, s1=start, s2=stop:
                  e.matmul(o, l, r, start=s1, stop=s2),
                  reads=reads, writes=writes, tag=tag)

    def transp(self, out_ap, in_ap, ident_ap, reads, writes, tag=""):
        self.p.op("pe",
                  lambda e, o=out_ap, i=in_ap, d=ident_ap: e.transpose(o, i, d),
                  reads=reads, writes=writes, tag=tag)

    def act(self, fn, reads, writes, tag=""):
        self.p.op("act", fn, reads=reads, writes=writes, tag=tag)

    def dve(self, fn, reads, writes, tag=""):
        self.p.op("dve", fn, reads=reads, writes=writes, tag=tag)

    def load_consts(self, es, names, consts_meta):
        for name in names:
            shape, dt = consts_meta[name]
            sb = self.sb(es, "s_" + name, shape, dt)
            self.dma("sp", sb[:], self.d[name], reads=(), writes=("s_" + name,),
                     tag="ldc")



def emit_load_x(cx, b):
    idf = cx.t["s_c_ident_f32"]
    x_in = cx.d["x_in"]
    with ExitStack() as es:
        stg = cx.sb(es, f"ldx{b}_stg", [128, C], F32)
        pst = cx.ps(es, f"ldx{b}_ps", [128, 128], F32)
        kst, kps = f"ldx{b}_stg", f"ldx{b}_ps"
        for tt in range(TT):
            cx.dma("sp", stg[:], x_in[b, tt * 128:(tt + 1) * 128, :],
                   reads=(), writes=(kst,), tag="ldx")
            for ci in range(CC):
                cx.transp(pst[:], stg[:, ci * 128:(ci + 1) * 128], idf[:],
                          reads=(kst, "s_c_ident_f32"), writes=(kps,))
                cx.dve(lambda e, o=cx.t[f"X{ci}"][:, tt * 128:(tt + 1) * 128],
                       i=pst[:]: e.tensor_copy(o, i),
                       reads=(kps,), writes=(f"X{ci}",))
        for ci in range(CC):
            cx.dve(lambda e, o=cx.t[f"XBF{ci}"][:], i=cx.t[f"X{ci}"][:]:
                   e.tensor_copy(o, i),
                   reads=(f"X{ci}",), writes=(f"XBF{ci}",), tag="shadow")
        if _dbg("dump_sm") and b == 1:
            cx.dve(lambda e, o=cx.t["X7"][:, 0:512],
                   i=cx.t["X0"][:, 0:512]: e.tensor_copy(o, i),
                   reads=("X0",), writes=("X7",))
        cx.p.barrier()


def emit_store_x(cx, b):
    out = cx.d["out"]
    idf = cx.t["s_c_ident_f32"]
    with ExitStack() as es:
        stg = cx.sb(es, f"stx{b}_stg", [128, C], F32)
        pst = cx.ps(es, f"stx{b}_ps", [128, 128], F32)
        kst, kps = f"stx{b}_stg", f"stx{b}_ps"
        for tt in range(TT):
            for ci in range(CC):
                cx.transp(pst[:], cx.t[f"X{ci}"][:, tt * 128:(tt + 1) * 128],
                          idf[:],
                          reads=(f"X{ci}", "s_c_ident_f32"), writes=(kps,))
                cx.dve(lambda e, o=stg[:, ci * 128:(ci + 1) * 128], i=pst[:]:
                       e.tensor_copy(o, i),
                       reads=(kps,), writes=(kst,))
            cx.dma("sp", out[b, tt * 128:(tt + 1) * 128, :], stg[:],
                   reads=(kst,), writes=(), tag="stx")
        cx.p.barrier()


def emit_postln(cx, uid):
    u = uid
    with ExitStack() as es:
        sq = cx.sb(es, f"ln{u}_sq", [128, 512], F32)
        t1 = cx.sb(es, f"ln{u}_t1", [128, 512], F32)
        rows = cx.sb(es, f"ln{u}_rows", [1, 512 * 4], F32)
        psx = cx.ps(es, f"ln{u}_psx", [1, 512], F32)
        psq = cx.ps(es, f"ln{u}_psq", [1, 512], F32)
        psa = cx.ps(es, f"ln{u}_psa", [128, 512], F32)
        psb = cx.ps(es, f"ln{u}_psb", [128, 512], F32)
        onec = cx.t["s_c_ones_col"]
        oner = cx.sb(es, f"ln{u}_ones1", [1, 128], F32)
        cx.dma("sp", oner[:], cx.d["c_ones1"], reads=(), writes=(f"ln{u}_o1",))
        kx, kq, ka, kb = f"ln{u}_psx", f"ln{u}_psq", f"ln{u}_psa", f"ln{u}_psb"
        kr = f"ln{u}_rows"
        for tg in range(TG):
            sl = slice(tg * 512, tg * 512 + 512)
            for ci in range(CC):
                cx.mm(psx[:], onec[:, 0:1], cx.t[f"X{ci}"][:, sl],
                      start=(ci == 0), stop=(ci == CC - 1),
                      reads=(f"X{ci}", "s_c_ones_col"), writes=(kx,))
                cx.dve(lambda e, o=sq[:], i=cx.t[f"X{ci}"][:, sl]:
                       e.tensor_mul(o, i, i),
                       reads=(f"X{ci}",), writes=(f"ln{u}_sq",))
                cx.mm(psq[:], onec[:, 0:1], sq[:],
                      start=(ci == 0), stop=(ci == CC - 1),
                      reads=(f"ln{u}_sq", "s_c_ones_col"), writes=(kq,))
            mean = rows[0:1, 0:512]
            var = rows[0:1, 512:1024]
            rstd = rows[0:1, 1024:1536]
            brow = rows[0:1, 1536:2048]
            cx.dve(lambda e, o=mean, i=psx[:]:
                   e.tensor_scalar_mul(o, i, 1.0 / C),
                   reads=(kx,), writes=(kr,))
            cx.dve(lambda e, o=var, a=mean:
                   e.scalar_tensor_tensor(out=o, in0=a, scalar=-1.0, in1=a,
                                          op0=ALU.mult, op1=ALU.mult),
                   reads=(kr,), writes=(kr,))
            cx.dve(lambda e, o=var, a=psq[:], b2=var:
                   e.scalar_tensor_tensor(out=o, in0=a, scalar=1.0 / C, in1=b2,
                                          op0=ALU.mult, op1=ALU.add),
                   reads=(kq, kr), writes=(kr,))
            cx.act(lambda e, o=rstd, i=var,
                   ep=cx.t["s_c_eps"][0:1, 0:1]:
                   e.activation(out=o, in_=i, func=AF.Ln, bias=ep),
                   reads=(kr, "s_c_eps"), writes=(kr,))
            cx.act(lambda e, o=rstd, i=rstd:
                   e.activation(out=o, in_=i, func=AF.Exp, scale=-0.5),
                   reads=(kr,), writes=(kr,))
            cx.dve(lambda e, o=brow, a=mean, b2=rstd:
                   e.scalar_tensor_tensor(out=o, in0=a, scalar=-1.0, in1=b2,
                                          op0=ALU.mult, op1=ALU.mult),
                   reads=(kr,), writes=(kr,))
            cx.mm(psa[:], oner[0:1, :], rstd, start=True, stop=True,
                  reads=(kr, f"ln{u}_o1"), writes=(ka,))
            cx.mm(psb[:], oner[0:1, :], brow, start=True, stop=True,
                  reads=(kr, f"ln{u}_o1"), writes=(kb,))
            for ci in range(CC):
                cx.dve(lambda e, o=t1[:], a=cx.t[f"X{ci}"][:, sl], bb=psa[:]:
                       e.tensor_mul(o, a, bb),
                       reads=(f"X{ci}", ka), writes=(f"ln{u}_t1",))
                cx.dve(lambda e, o=cx.t[f"X{ci}"][:, sl], a=t1[:], bb=psb[:]:
                       e.tensor_add(o, a, bb),
                       reads=(f"ln{u}_t1", kb), writes=(f"X{ci}",))
                cx.dve(lambda e, o=cx.t[f"XBF{ci}"][:, sl],
                       i=cx.t[f"X{ci}"][:, sl]: e.tensor_copy(o, i),
                       reads=(f"X{ci}",), writes=(f"XBF{ci}",))
        cx.p.barrier()


def load_w_chunks_f32(cx, es, pref, dram_ap, nrows, ncols, row0=0):
    n = nrows // 128
    names = []
    for i in range(n):
        nm = f"{pref}{i}"
        sbt = cx.sb(es, nm, [128, ncols], F32)
        cx.dma("sp", sbt[:], dram_ap[row0 + i * 128: row0 + (i + 1) * 128, :],
               reads=(), writes=(nm,))
        names.append(nm)
    return names


def transpose_chunks(cx, es, dst_pref, src_pref, n_src, src_cols, dtype,
                     ps_name, dst_tensors=None):
    nd = src_cols // 128
    ident = cx.t["s_c_ident_f32" if dtype == F32 else "s_c_ident_bf"]
    ident_k = "s_c_ident_f32" if dtype == F32 else "s_c_ident_bf"
    pst = cx.t[ps_name]
    pk = ps_name
    if dst_tensors is None:
        dst_tensors = []
        for j in range(nd):
            nm = f"{dst_pref}{j}"
            cx.sb(es, nm, [128, n_src * 128], dtype)
            dst_tensors.append(nm)
    for i in range(n_src):
        for j in range(nd):
            src = cx.t[f"{src_pref}{i}"]
            cx.transp(pst[:], src[:, j * 128:(j + 1) * 128], ident[:],
                      reads=(f"{src_pref}{i}", ident_k), writes=(pk,))
            dstnm = dst_tensors[j]
            cx.dve(lambda e, o=cx.t[dstnm][:, i * 128:(i + 1) * 128], p=pst[:]:
                   e.tensor_copy(o, p),
                   reads=(pk,), writes=(dstnm,))
    return dst_tensors



def emit_l0attn(cx, b):
    p = cx.p
    u = f"a{b}"
    with ExitStack() as es0:
        k2 = [cx.sb(es0, f"{u}_k2{i}", [128, 512], BF16) for i in range(CC)]
        m2 = [cx.sb(es0, f"{u}_m2{i}", [128, C], BF16) for i in range(4)]
        ckv_fm = [cx.sb(es0, f"{u}_ckvf{i}", [128, T], BF16) for i in range(4)]
        ckv_tm = [cx.sb(es0, f"{u}_ckvt{i}", [128, 512], BF16) for i in range(TT)]
        qf = [cx.sb(es0, f"{u}_qf{i}", [128, T], BF16) for i in range(CC)]
        ps512 = cx.ps(es0, f"{u}_ps512", [128, 512], F32)
        pstb = cx.ps(es0, f"{u}_pstb", [128, 128], BF16)
        k512, kstb = f"{u}_ps512", f"{u}_pstb"
        idb = cx.t["s_c_ident_bf"]
        kf = [f"{u}_ckvf{i}" for i in range(4)]
        kt = [f"{u}_ckvt{i}" for i in range(TT)]
        kq = [f"{u}_qf{i}" for i in range(CC)]

        def tile_T(dst_ap, dst_key, src_ap, src_key):
            cx.transp(pstb[:], src_ap, idb[:], reads=(src_key, "s_c_ident_bf"),
                      writes=(kstb,))
            cx.dve(lambda e, o=dst_ap, i=pstb[:]: e.tensor_copy(o, i),
                   reads=(kstb,), writes=(dst_key,))

        with ExitStack() as es1:
            pbf = [cx.sb(es1, f"{u}_pbf{i}", [128, C], BF16) for i in range(CC)]
            wdkvT = [cx.sb(es1, f"{u}_dkvT{i}", [128, 512], BF16)
                     for i in range(CC)]
            kpb = [f"{u}_pbf{i}" for i in range(CC)]
            kdk = [f"{u}_dkvT{i}" for i in range(CC)]

            for i in range(CC):
                cx.dma("pool", ckv_tm[i][:],
                       cx.d["L0_attn_W_uq"][i * 128:(i + 1) * 128, :],
                       reads=(kt[i],), writes=(kt[i],))
                cx.dma("pool", ckv_tm[8 + i][:],
                       cx.d["L0_attn_W_uk"][i * 128:(i + 1) * 128, :],
                       reads=(kt[8 + i],), writes=(kt[8 + i],))
            for uc in range(4):
                for q in range(CC):
                    cx.mm(ps512[:], ckv_tm[q][:, uc * 128:(uc + 1) * 128],
                          ckv_tm[8 + q][:],
                          start=(q == 0), stop=(q == CC - 1),
                          reads=(kt[q], kt[8 + q]), writes=(k512,))
                cx.dve(lambda e, o=ckv_fm[uc][:, 0:512], i=ps512[:]:
                       e.tensor_copy(o, i),
                       reads=(k512,), writes=(kf[uc],))
            for i in range(CC):
                for j in range(4):
                    tile_T(qf[j][:, i * 128:(i + 1) * 128], kq[j],
                           ckv_tm[i][:, j * 128:(j + 1) * 128], kt[i])
            for j in range(4):
                cx.dma("pool", qf[4 + j][:, 0:1024],
                       cx.d["L0_attn_W_dq"][j * 128:(j + 1) * 128, :],
                       reads=(kq[4 + j],), writes=(kq[4 + j],))
            for cc2 in range(CC):
                for g in range(2):
                    for q in range(4):
                        cx.mm(ps512[:],
                              qf[4 + q][:, cc2 * 128:(cc2 + 1) * 128],
                              qf[q][:, g * 512:(g + 1) * 512],
                              start=(q == 0), stop=(q == 3),
                              reads=(kq[4 + q], kq[q]), writes=(k512,))
                    cx.dve(lambda e, o=pbf[cc2][:, g * 512:(g + 1) * 512],
                           i=ps512[:]: e.tensor_copy(o, i),
                           reads=(k512,), writes=(kpb[cc2],))
                for q in range(4):
                    cx.mm(ps512[:],
                          qf[4 + q][:, cc2 * 128:(cc2 + 1) * 128],
                          ckv_fm[q][:, 0:512],
                          start=(q == 0), stop=(q == 3),
                          reads=(kq[4 + q], kf[q]), writes=(k512,))
                cx.dve(lambda e, o=k2[cc2][:], i=ps512[:]: e.tensor_copy(o, i),
                       reads=(k512,), writes=(f"{u}_k2{cc2}",))
            for i in range(CC):
                cx.dma("pool", ckv_tm[i][:],
                       cx.d["L0_attn_W_uv"][i * 128:(i + 1) * 128, :],
                       reads=(kt[i],), writes=(kt[i],))
            for half in range(2):
                for j in range(4):
                    cx.dma("pool", qf[j][:, 0:1024],
                           cx.d["L0_attn_W_o"][half * 512 + j * 128:
                                               half * 512 + (j + 1) * 128, :],
                           reads=(kq[j],), writes=(kq[j],))
                for i in range(4):
                    for j in range(CC):
                        tile_T(ckv_tm[8 + j][:, i * 128:(i + 1) * 128], kt[8 + j],
                               qf[i][:, j * 128:(j + 1) * 128], kq[i])
                for lc in range(4):
                    for q in range(CC):
                        cx.mm(ps512[:],
                              ckv_tm[q][:, lc * 128:(lc + 1) * 128],
                              ckv_tm[8 + q][:],
                              start=(q == 0), stop=(q == CC - 1),
                              reads=(kt[q], kt[8 + q]), writes=(k512,))
                    cx.dve(lambda e, o=m2[lc][:, half * 512:(half + 1) * 512],
                           i=ps512[:]: e.tensor_copy(o, i),
                           reads=(k512,), writes=(f"{u}_m2{lc}",))
            for j in range(4):
                cx.dma("pool", qf[4 + j][:, 0:1024],
                       cx.d["L0_attn_W_dkv"][j * 128:(j + 1) * 128, :],
                       reads=(kq[4 + j],), writes=(kq[4 + j],))
            for i in range(4):
                for j in range(CC):
                    tile_T(wdkvT[j][:, i * 128:(i + 1) * 128], kdk[j],
                           qf[4 + i][:, j * 128:(j + 1) * 128], kq[4 + i])

            for li in range(4):
                for tg in range(TG):
                    for q in range(CC):
                        cx.mm(ps512[:],
                              wdkvT[q][:, li * 128:(li + 1) * 128],
                              cx.t[f"XBF{q}"][:, tg * 512:(tg + 1) * 512],
                              start=(q == 0), stop=(q == CC - 1),
                              reads=(kdk[q], f"XBF{q}"), writes=(k512,))
                    cx.dve(lambda e, o=ckv_fm[li][:, tg * 512:(tg + 1) * 512],
                           i=ps512[:]: e.tensor_copy(o, i),
                           reads=(k512,), writes=(kf[li],))
            if _dbg("ckv"):
                for li in range(4):
                    cx.dve(lambda e, o=cx.t[f"X{li}"][:], i=ckv_fm[li][:]:
                           e.tensor_copy(o, i),
                           reads=(kf[li],), writes=(f"X{li}",))
                for li in range(4, 8):
                    cx.dve(lambda e, o=cx.t[f"X{li}"][:]: e.memset(o, 0.0),
                           reads=(), writes=(f"X{li}",))
                p.barrier()
                return
            for cq in range(CC):
                for tg in range(TG):
                    for q in range(CC):
                        cx.mm(ps512[:],
                              pbf[q][:, cq * 128:(cq + 1) * 128],
                              cx.t[f"XBF{q}"][:, tg * 512:(tg + 1) * 512],
                              start=(q == 0), stop=(q == CC - 1),
                              reads=(kpb[q], f"XBF{q}"), writes=(k512,))
                    cx.dve(lambda e, o=qf[cq][:, tg * 512:(tg + 1) * 512],
                           i=ps512[:]: e.tensor_copy(o, i),
                           reads=(k512,), writes=(kq[cq],))
            if _dbg("dump_qf"):
                for cc2 in range(CC):
                    cx.dve(lambda e, o=cx.t[f"X{cc2}"][:], i=qf[cc2][:]:
                           e.tensor_copy(o, i),
                           reads=(f"{u}_qf{cc2}",), writes=(f"X{cc2}",))
                p.barrier()
                return
            if _dbg("q"):
                for cc2 in range(CC):
                    cx.dve(lambda e, o=cx.t[f"X{cc2}"][:], i=qf[cc2][:]:
                           e.tensor_copy(o, i),
                           reads=(f"{u}_qf{cc2}",), writes=(f"X{cc2}",))
                p.barrier()
                return
            p.barrier()

        for li in range(4):
            for tt in range(TT):
                tile_T(ckv_tm[tt][:, li * 128:(li + 1) * 128], kt[tt],
                       ckv_fm[li][:, tt * 128:(tt + 1) * 128], kf[li])

        with ExitStack() as es2:
            qlat = [cx.sb(es2, f"{u}_ql{i}", [128, T], BF16) for i in range(4)]
            vtm = [cx.sb(es2, f"{u}_vtm{i}", [128, 64], BF16) for i in range(TT)]
            att = cx.sb(es2, f"{u}_att", [128, T], BF16)
            attT = [cx.sb(es2, f"{u}_attT{i}", [128, 128], BF16)
                    for i in range(TT)]
            small = cx.sb(es2, f"{u}_small", [128, 4], F32)
            pssc = cx.ps(es2, f"{u}_pssc", [128, T], F32)
            psy = cx.ps(es2, f"{u}_psy", [64, 128], F32)
            ksc, ky, ksm = f"{u}_pssc", f"{u}_psy", f"{u}_small"

            if _dbg("dump_sm"):
                cx.dve(lambda e, o=cx.t["X6"][:, 0:512],
                       i=cx.t["X0"][:, 0:512]: e.tensor_copy(o, i),
                       reads=("X0",), writes=("X6",))
            n_heads = 1 if _dbg("h0only") else 16
            for h in range(n_heads):
                hc, hb = h // 2, (h % 2) * 64
                for tt in range(TT):
                    for li in range(4):
                        cx.mm(ps512[:, 0:64],
                              ckv_fm[li][:, tt * 128:(tt + 1) * 128],
                              m2[li][:, h * 64:(h + 1) * 64],
                              start=(li == 0), stop=(li == 3),
                              reads=(kf[li], f"{u}_m2{li}"), writes=(k512,))
                    cx.dve(lambda e, o=vtm[tt][:], i=ps512[:, 0:64]:
                           e.tensor_copy(o, i),
                           reads=(k512,), writes=(f"{u}_vtm{tt}",))
                for li in range(4):
                    for tg in range(TG):
                        cx.mm(ps512[:],
                              k2[hc][hb:hb + 64, li * 128:(li + 1) * 128],
                              qf[hc][hb:hb + 64, tg * 512:(tg + 1) * 512],
                              start=True, stop=True,
                              reads=(f"{u}_k2{hc}", kq[hc]), writes=(k512,))
                        cx.dve(lambda e, o=qlat[li][:, tg * 512:(tg + 1) * 512],
                               i=ps512[:]: e.tensor_copy(o, i),
                               reads=(k512,), writes=(f"{u}_ql{li}",))
                for qi in range(TT):
                    nk = (qi + 1) * 128
                    ngroups = (nk + 511) // 512
                    for li in range(4):
                        for g in range(ngroups):
                            n = min(512, nk - g * 512)
                            cx.mm(pssc[:, g * 512:g * 512 + n],
                                  qlat[li][:, qi * 128:(qi + 1) * 128],
                                  ckv_fm[li][:, g * 512:g * 512 + n],
                                  start=(li == 0), stop=(li == 3),
                                  reads=(f"{u}_ql{li}", kf[li]),
                                  writes=(ksc,))
                    cx.dve(lambda e, o=pssc[:, qi * 128:qi * 128 + 128],
                           m=cx.t["s_c_tri_causal"][:]: e.tensor_add(o, o, m),
                           reads=(ksc, "s_c_tri_causal"), writes=(ksc,))
                    cx.dve(lambda e, o=small[:, 0:1], i=pssc[:, 0:nk]:
                           e.reduce_max(o, i, axis=AX.X),
                           reads=(ksc,), writes=(ksm,))
                    cx.dve(lambda e, o=small[:, 1:2], i=small[:, 0:1]:
                           e.tensor_scalar_mul(o, i, -0.125),
                           reads=(ksm,), writes=(ksm,))
                    cx.act(lambda e, o=att[:, 0:nk], i=pssc[:, 0:nk],
                           bb=small[:, 1:2]:
                           e.activation(out=o, in_=i, func=AF.Exp, scale=0.125,
                                        bias=bb),
                           reads=(ksc, ksm), writes=(f"{u}_att",))
                    cx.dve(lambda e, o=small[:, 2:3], i=att[:, 0:nk]:
                           e.reduce_sum(o, i, axis=AX.X),
                           reads=(f"{u}_att",), writes=(ksm,))
                    cx.dve(lambda e, o=small[:, 3:4], i=small[:, 2:3]:
                           e.reciprocal(o, i),
                           reads=(ksm,), writes=(ksm,))
                    cx.dve(lambda e, o=att[:, 0:nk], i=att[:, 0:nk],
                           s=small[:, 3:4]:
                           e.tensor_scalar(out=o, in0=i, scalar1=s, scalar2=None,
                                           op0=ALU.mult),
                           reads=(f"{u}_att", ksm), writes=(f"{u}_att",))
                    if _dbg("dump_sm") and h == 0:
                        cx.dve(lambda e, o=cx.t["X3"][:, qi * 4:(qi + 1) * 4],
                               i=small[:, 0:4]: e.tensor_copy(o, i),
                               reads=(ksm,), writes=("X3",))
                        if qi == 2:
                            cx.dve(lambda e, o=cx.t["X2"][:, 0:384],
                                   i=att[:, 0:384]: e.tensor_copy(o, i),
                                   reads=(f"{u}_att",), writes=("X2",))
                    for kti in range(qi + 1):
                        tile_T(attT[kti][:], f"{u}_attT{kti}",
                               att[:, kti * 128:(kti + 1) * 128], f"{u}_att")
                    if _dbg("dump_sm") and h == 0 and qi == 2:
                        for kti in range(3):
                            cx.dve(lambda e,
                                   o=cx.t["X4"][:, kti * 128:(kti + 1) * 128],
                                   i=attT[kti][:]: e.tensor_copy(o, i),
                                   reads=(f"{u}_attT{kti}",), writes=("X4",))
                    for kti in range(qi + 1):
                        cx.mm(psy[:], vtm[kti][:], attT[kti][:],
                              start=(kti == 0), stop=(kti == qi),
                              reads=(f"{u}_vtm{kti}", f"{u}_attT{kti}"),
                              writes=(ky,))
                    if _dbg("dump_sm") and h == 0 and qi == 2:
                        cx.dve(lambda e, o=cx.t["X5"][0:64, 0:128],
                               i=psy[0:64, :]: e.tensor_copy(o, i),
                               reads=(ky,), writes=("X5",))
                    cx.dve(lambda e,
                           o=cx.t[f"X{hc}"][hb:hb + 64,
                                            qi * 128:(qi + 1) * 128],
                           i=psy[0:64, :]: e.tensor_add(o, o, i),
                           reads=(ky, f"X{hc}"), writes=(f"X{hc}",))
                if _dbg("dump_h0"):
                    for li in range(4):
                        cx.dve(lambda e, o=cx.t[f"X{4 + li}"][:], i=qlat[li][:]:
                               e.tensor_copy(o, i),
                               reads=(f"{u}_ql{li}",), writes=(f"X{4 + li}",))
                    for tt in range(TT):
                        cx.dve(lambda e, o=cx.t["X1"][0:128,
                                                     tt * 64:(tt + 1) * 64],
                               i=vtm[tt][:]: e.tensor_copy(o, i),
                               reads=(f"{u}_vtm{tt}",), writes=("X1",))
            p.barrier()
    if (_dbg("attnres") or _dbg("noln")):
        return
    emit_postln(cx, f"{u}ln")



def emit_preconvert_moe(cx):
    p = cx.p
    w1d = cx.d["L0_ffn_W1"]
    w2d = cx.d["L0_ffn_W2"]
    s1 = cx.d["scr_w1bf"]
    s2 = cx.d["scr_w2bf"]
    F = 4096
    with ExitStack() as es:
        tmp = [cx.sb(es, f"pcv_t{i}", [128, F], BF16) for i in range(2)]
        for e in range(4):
            if DBG != "pcv_w2only":
                for ccx in range(CC):
                    t = ccx % 2
                    k = f"pcv_t{t}"
                    srcw = w1d.rearrange("e (cc p) f -> e cc p f", p=128)[e, ccx]
                    cx.dma("pool", tmp[t][:], srcw, reads=(k,), writes=(k,))
                    cx.dma("sp", s1[e, :, ccx * F:(ccx + 1) * F], tmp[t][:],
                           reads=(k,), writes=(k,))
            if DBG != "pcv_w1only":
                for fb in range(8):
                    t = fb % 2
                    k = f"pcv_t{t}"
                    srcw = w2d.rearrange("e (fb fc p) c -> e fb p fc c", fb=8, p=128)[e, fb]
                    dstw = tmp[t][:].rearrange("p (fc c) -> p fc c", fc=4)
                    cx.dma("pool", dstw, srcw, reads=(k,), writes=(k,))
                    cx.dma("sp", s2[e, fb], tmp[t][:], reads=(k,), writes=(k,))
            if _dbg("pcv_bar"):
                p.barrier()
        p.barrier()


def emit_preconvert_l1(cx):
    p = cx.p
    w1d = cx.d["L1_ffn_w1"]
    w2d = cx.d["L1_ffn_w2"]
    s1 = cx.d["scr_l1w1"]
    s2 = cx.d["scr_l1w2"]
    HID = 2730
    with ExitStack() as es:
        tmp = [cx.sb(es, f"pl1_t{i}", [128, CC * 128], BF16) for i in range(2)]
        tmp2 = [cx.sb(es, f"pl1_u{i}", [128, 22 * 128], BF16) for i in range(2)]
        for l in range(2):
            for half in range(2):
                for fi in range(22):
                    nf = 128 if fi < 21 else 42
                    t = fi % 2
                    k = f"pl1_t{t}"
                    col0 = half * HID + fi * 128
                    srcw = w1d[l][:, col0:col0 + nf].rearrange(
                        "(cc p) f -> p cc f", p=128)
                    dst = tmp[t][:].rearrange("p (cc f) -> p cc f", cc=CC)
                    cx.dma("pool", dst[:, :, 0:nf], srcw, reads=(k,),
                           writes=(k,))
                    cx.dma("sp", s1[l, half * 22 + fi], tmp[t][:],
                           reads=(k,), writes=(k,))
            for ccx in range(CC):
                t = ccx % 2
                k = f"pl1_u{t}"
                src21 = w2d[l][:, ccx * 128:(ccx + 1) * 128]
                srcm = src21[0:21 * 128, :].rearrange("(fi p) c -> p fi c", p=128)
                dstm = tmp2[t][:, 0:21 * 128].rearrange("p (fi c) -> p fi c",
                                                        fi=21)
                cx.dve(lambda e, o=tmp2[t][:, 21 * 128:22 * 128]:
                       e.memset(o, 0.0),
                       reads=(k,), writes=(k,))
                cx.dma("pool", dstm, srcm, reads=(k,), writes=(k,))
                cx.dma("pool", tmp2[t][0:42, 21 * 128:22 * 128],
                       src21[21 * 128:2730, :], reads=(k,), writes=(k,))
                cx.dma("sp", s2[l, ccx], tmp2[t][:], reads=(k,), writes=(k,))
        p.barrier()


def emit_l0ffn(cx, b):
    if _dbg("skip_l0ffn"):
        return
    p = cx.p
    u = f"m{b}"
    E, R, F = 4, 4, 4096
    if DBG:
        for part in str(DBG).split(","):
            if part.startswith("moe_") and part.endswith("r") and part[4:-1].isdigit():
                R = int(part[4:-1])
    NFC = F // 128
    scr = cx.d["scr_xouter"]
    with ExitStack() as es:
        for ci in range(CC):
            cx.dma("sp", scr[ci * 128:(ci + 1) * 128, :], cx.t[f"X{ci}"][:],
                   reads=(f"X{ci}",), writes=(f"scrx{ci}",))
        wg = cx.sb(es, f"{u}_wg", [128, 4 * CC * 4], F32)
        srcg = cx.d["L0_ffn_Wg"].rearrange("r (cc p) e -> p r cc e", p=128)
        dstg = wg[:].rearrange("p (r cc e) -> p r cc e", r=4, cc=CC)
        cx.dma("sp", dstg, srcg, reads=(), writes=(f"{u}_wg",))
        w1 = cx.sb(es, f"{u}_w1", [128, CC * F], BF16)
        w2 = [cx.sb(es, f"{u}_w2_{i}", [128, 2 * 1024], BF16) for i in range(2)]
        hmid = [cx.sb(es, f"{u}_hm{i}", [128, 512], BF16) for i in range(NFC)]
        cw = cx.sb(es, f"{u}_cw", [128, TT * 4], F32)
        diag = cx.sb(es, f"{u}_diag", [128, 128], BF16)
        gs = cx.sb(es, f"{u}_gs", [128, 32], F32)
        ps8 = [cx.ps(es, f"{u}_ps8_{i}", [128, 512], F32) for i in range(8)]
        kp = [f"{u}_ps8_{i}" for i in range(8)]
        s1 = cx.d["scr_w1bf"]
        s2 = cx.d["scr_w2bf"]
        b2sb = cx.sb(es, f"{u}_b2sb", [4, C], BF16)
        cwt4 = cx.sb(es, f"{u}_cwt4", [4, 512], BF16)
        cx.dma("pool", b2sb[:], cx.d["L0_ffn_b2"], reads=(), writes=(f"{u}_b2sb",))
        kw1 = f"{u}_w1"

        def load_w1(e_idx):
            for half in range(2):
                cx.dma("sp", w1[:, half * 4 * F:(half + 1) * 4 * F],
                       s1[e_idx, :, half * 4 * F:(half + 1) * 4 * F],
                       reads=(), writes=(kw1,), tag="w1ld")

        def load_w2(e_idx, fb, half, buf):
            cx.dma("sp", w2[buf][:],
                   s2[e_idx, fb, :, half * 2048:(half + 1) * 2048],
                   reads=(), writes=(f"{u}_w2_{buf}",), tag="w2ld")

        load_w1(0)
        if _dbg("moe_w1"):
            for ci in range(CC):
                cx.dve(lambda e, o=cx.t[f"X{ci}"][:],
                       i=w1[:, ci * T:(ci + 1) * T]: e.tensor_copy(o, i),
                       reads=(kw1,), writes=(f"X{ci}",))
            p.barrier()
            return
        if _dbg("moe_w2"):
            load_w2(0, 0, 0)
            load_w2(0, 1, 1)
            for half in range(2):
                for ci in range(4):
                    xi = half * 4 + ci
                    cx.dve(lambda e, o=cx.t[f"X{xi}"][:, 0:1024],
                           i=w2[half][:, ci * 1024:(ci + 1) * 1024]:
                           e.tensor_copy(o, i),
                           reads=(f"{u}_w2_{half}",), writes=(f"X{xi}",))
                    cx.dve(lambda e, o=cx.t[f"X{xi}"][:, 1024:2048]:
                           e.memset(o, 0.0),
                           reads=(), writes=(f"X{xi}",))
            p.barrier()
            return
        for r in range(R):
            for tt in range(TT):
                for q in range(CC):
                    cx.mm(ps8[0][:, 0:4],
                          cx.t[f"X{q}"][:, tt * 128:(tt + 1) * 128],
                          wg[:, (r * CC + q) * 4:(r * CC + q + 1) * 4],
                          start=(q == 0), stop=(q == CC - 1),
                          reads=(f"X{q}", f"{u}_wg"), writes=(kp[0],))
                kgs = f"{u}_gs"
                g0 = gs
                cx.dve(lambda e, o=g0[:, 0:4], i=ps8[0][:, 0:4],
                       m=cx.t["s_pm_l0_bg"][:, r * 4:(r + 1) * 4]:
                       e.tensor_add(o, i, m),
                       reads=(kp[0], "s_pm_l0_bg"), writes=(kgs,))
                cx.dve(lambda e, o=g0[:, 4:5], i=g0[:, 0:4]:
                       e.reduce_max(o, i, axis=AX.X),
                       reads=(kgs,), writes=(kgs,))
                cx.dve(lambda e, o=g0[:, 5:6], i=g0[:, 4:5]:
                       e.tensor_scalar_mul(o, i, -1.0),
                       reads=(kgs,), writes=(kgs,))
                cx.dve(lambda e, o=g0[:, 8:12], i=g0[:, 0:4], s=g0[:, 4:5]:
                       e.tensor_scalar(out=o, in0=i, scalar1=s, scalar2=None,
                                       op0=ALU.is_ge),
                       reads=(kgs,), writes=(kgs,))
                cx.dve(lambda e, o=g0[:, 12:16], i=g0[:, 8:12], m=g0[:, 0:4]:
                       e.scalar_tensor_tensor(out=o, in0=i, scalar=-1e30,
                                              in1=m, op0=ALU.mult, op1=ALU.add),
                       reads=(kgs,), writes=(kgs,))
                cx.dve(lambda e, o=g0[:, 6:7], i=g0[:, 12:16]:
                       e.reduce_max(o, i, axis=AX.X),
                       reads=(kgs,), writes=(kgs,))
                cx.act(lambda e, o=g0[:, 16:20], i=g0[:, 0:4], bb=g0[:, 5:6]:
                       e.activation(out=o, in_=i, func=AF.Exp, bias=bb),
                       reads=(kgs,), writes=(kgs,))
                cx.dve(lambda e, o=g0[:, 20:24], i=g0[:, 0:4], s=g0[:, 6:7]:
                       e.tensor_scalar(out=o, in0=i, scalar1=s, scalar2=None,
                                       op0=ALU.is_ge),
                       reads=(kgs,), writes=(kgs,))
                cx.dve(lambda e, o=g0[:, 16:20], a=g0[:, 16:20], b2=g0[:, 20:24]:
                       e.tensor_mul(o, a, b2),
                       reads=(kgs,), writes=(kgs,))
                cx.dve(lambda e, o=g0[:, 7:8], i=g0[:, 16:20]:
                       e.reduce_sum(o, i, axis=AX.X),
                       reads=(kgs,), writes=(kgs,))
                cx.dve(lambda e, o=g0[:, 24:25], i=g0[:, 7:8]:
                       e.reciprocal(o, i),
                       reads=(kgs,), writes=(kgs,))
                cx.dve(lambda e, o=cw[:, tt * 4:(tt + 1) * 4], i=g0[:, 16:20],
                       s=g0[:, 24:25]:
                       e.tensor_scalar(out=o, in0=i, scalar1=s, scalar2=None,
                                       op0=ALU.mult),
                       reads=(kgs,), writes=(f"{u}_cw",))
            if _dbg("moe_gate") and r == 0:
                cx.dve(lambda e, o=cx.t["X0"][:, 0:64], i=cw[:]:
                       e.tensor_copy(o, i),
                       reads=(f"{u}_cw",), writes=("X0",))
                cx.dve(lambda e, o=cx.t["X0"][:, 64:T]: e.memset(o, 0.0),
                       reads=(), writes=("X0",))
                for ci in range(1, CC):
                    cx.dve(lambda e, o=cx.t[f"X{ci}"][:]: e.memset(o, 0.0),
                           reads=(), writes=(f"X{ci}",))
                p.barrier()
                return
            nE = 1 if _dbg("moe_1e") else E
            for ei in range(nE):
                for tg in range(TG):
                    sl = slice(tg * 512, tg * 512 + 512)
                    for tt4 in range(4):
                        tt = tg * 4 + tt4
                        cx.dve(lambda e, o=diag[:],
                               i=cx.t["s_c_ident_bf"][:],
                               s=cw[:, tt * 4 + ei:tt * 4 + ei + 1]:
                               e.tensor_scalar(out=o, in0=i, scalar1=s,
                                               scalar2=None, op0=ALU.mult),
                               reads=(f"{u}_cw", "s_c_ident_bf"),
                               writes=(f"{u}_diag",))
                        cx.mm(ps8[2][:, tt4 * 128:(tt4 + 1) * 128],
                              cx.t["s_c_ones128_bf"][:], diag[:],
                              start=True, stop=True,
                              reads=(f"{u}_diag", "s_c_ones128_bf"),
                              writes=(kp[2],))
                    for fc in range(NFC):
                        pb = fc % 2
                        for q in range(CC):
                            cx.mm(ps8[pb][:],
                                  w1[:, q * F + fc * 128:q * F + (fc + 1) * 128],
                                  cx.t[f"XBF{q}"][:, sl],
                                  start=(q == 0), stop=(q == CC - 1),
                                  reads=(kw1, f"XBF{q}"),
                                  writes=(kp[pb],))
                        cx.act(lambda e, o=hmid[fc][:], i=ps8[pb][:],
                               bb=cx.t["s_pm_l0_b1"][:, ei * 32 + fc:
                                                     ei * 32 + fc + 1]:
                               e.activation(out=o, in_=i,
                                            func=AF.Gelu_apprx_tanh, bias=bb),
                               reads=(kp[pb], "s_pm_l0_b1"),
                               writes=(f"{u}_hm{fc}",))
                        cx.dve(lambda e, o=hmid[fc][:], i=hmid[fc][:],
                               cb=ps8[2][:]: e.tensor_mul(o, i, cb),
                               reads=(f"{u}_hm{fc}", kp[2]),
                               writes=(f"{u}_hm{fc}",))
                    if tg == TG - 1 and not (r == R - 1 and ei == E - 1):
                        load_w1((ei + 1) % E)
                    load_w2(ei, 0, 0, 0)
                    load_w2(ei, 0, 1, 1)
                    for hf in range(16):
                        fb, half = hf // 2, hf % 2
                        buf = hf % 2
                        for fci2 in range(2):
                            fc = fb * 4 + half * 2 + fci2
                            for ccx in range(CC):
                                cx.mm(ps8[ccx][:],
                                      w2[buf][:, fci2 * 1024 + ccx * 128:
                                              fci2 * 1024 + (ccx + 1) * 128],
                                      hmid[fc][:],
                                      start=(hf == 0 and fci2 == 0),
                                      stop=(hf == 15 and fci2 == 1),
                                      reads=(f"{u}_w2_{buf}", f"{u}_hm{fc}"),
                                      writes=(kp[ccx],))
                        if hf < 14:
                            nhf = hf + 2
                            load_w2(ei, nhf // 2, nhf % 2, buf)
                    for ccx in range(CC):
                        cx.dve(lambda e, o=cx.t[f"X{ccx}"][:, sl],
                               i=ps8[ccx][:]: e.tensor_add(o, o, i),
                               reads=(kp[ccx], f"X{ccx}"),
                               writes=(f"X{ccx}",))
            for tg in range(TG):
                for tt4 in range(4):
                    tt = tg * 4 + tt4
                    cx.transp(ps8[4][0:4, 0:128], cw[:, tt * 4:(tt + 1) * 4],
                              cx.t["s_c_ident_f32"][:],
                              reads=(f"{u}_cw", "s_c_ident_f32"),
                              writes=(kp[4],))
                    cx.dve(lambda e, o=cwt4[0:4, tt4 * 128:(tt4 + 1) * 128],
                           i=ps8[4][0:4, 0:128]: e.tensor_copy(o, i),
                           reads=(kp[4],), writes=(f"{u}_cwt4",))
                for ccx in range(CC):
                    cx.mm(ps8[5][:], b2sb[0:4, ccx * 128:(ccx + 1) * 128],
                          cwt4[0:4, :], start=True, stop=True,
                          reads=(f"{u}_b2sb", f"{u}_cwt4"), writes=(kp[5],))
                    cx.dve(lambda e,
                           o=cx.t[f"X{ccx}"][:, tg * 512:(tg + 1) * 512],
                           i=ps8[5][:]: e.tensor_add(o, o, i),
                           reads=(kp[5], f"X{ccx}"), writes=(f"X{ccx}",))
            if (_dbg("moe_1e") and r == R - 1) or (
                    DBG and any(pt.startswith("moe_") and pt.endswith("r")
                                and pt[4:-1].isdigit()
                                for pt in str(DBG).split(",")) and r == R - 1):
                p.barrier()
                return
            if r < R - 1:
                for ci in range(CC):
                    cx.dve(lambda e, o=cx.t[f"XBF{ci}"][:], i=cx.t[f"X{ci}"][:]:
                           e.tensor_copy(o, i),
                           reads=(f"X{ci}",), writes=(f"XBF{ci}",))
        for ci in range(CC):
            cx.p.op("pool",
                    lambda e, o=cx.t[f"X{ci}"][:],
                    i=scr[ci * 128:(ci + 1) * 128, :]:
                    e.dma_start(out=o, in_=i, accum_op=ALU.add),
                    reads=(f"scrx{ci}", f"X{ci}"), writes=(f"X{ci}",),
                    dma=True, tag="xreload")
        p.barrier()
    emit_postln(cx, f"{u}ln")



def emit_block(cx, blk, b):
    if blk == "lnonly":
        emit_postln(cx, f"dbgln{b}")
    elif blk == "l0attn":
        emit_l0attn(cx, b)
    elif blk == "l0ffn":
        emit_l0ffn(cx, b)
    elif blk == "l1attn":
        emit_l1attn(cx, b)
    elif blk == "l1ffn":
        emit_l1ffn(cx, b)
    else:
        raise ValueError(blk)


def emit_l1attn(cx, b):
    p = cx.p
    u = f"la{b}"
    LCH = [128, 128, 32]
    with ExitStack() as es0:
        latw = cx.sb(es0, f"{u}_latw", [128, CC * 288], BF16)
        srcw = cx.d["L1_attn_latent_w"].rearrange("(cc p) l -> p cc l", p=128)
        cx.dma("pool", latw[:].rearrange("p (cc l) -> p cc l", cc=CC), srcw,
               reads=(), writes=(f"{u}_latw",))
        kv_fm = [cx.sb(es0, f"{u}_kvf{i}", [LCH[i], T], BF16) for i in range(3)]
        kv_tm = [cx.sb(es0, f"{u}_kvt{i}", [128, 288], BF16) for i in range(TT)]
        wdw = [cx.sb(es0, f"{u}_wdw{i}", [128, CC * 288], BF16) for i in range(2)]
        outw = [cx.sb(es0, f"{u}_outw{i}", [128, 3 * C], BF16) for i in range(2)]
        q_fm = [cx.sb(es0, f"{u}_qf{i}", [LCH[i], T], BF16) for i in range(3)]
        ylat = [cx.sb(es0, f"{u}_yl{i}", [LCH[i], T], BF16) for i in range(3)]
        att = cx.sb(es0, f"{u}_att", [128, T], BF16)
        attT = [cx.sb(es0, f"{u}_attT{i}", [128, 128], BF16) for i in range(TT)]
        small = cx.sb(es0, f"{u}_small", [128, 4], F32)
        ps512 = cx.ps(es0, f"{u}_ps512", [128, 512], F32)
        pstb = cx.ps(es0, f"{u}_pstb", [128, 128], BF16)
        pssc = cx.ps(es0, f"{u}_pssc", [128, T], F32)
        psyl = cx.ps(es0, f"{u}_psyl", [128, 128], F32)
        k512, kstb, ksc, kyl = (f"{u}_ps512", f"{u}_pstb", f"{u}_pssc",
                                f"{u}_psyl")
        ksm = f"{u}_small"
        kf = [f"{u}_kvf{i}" for i in range(3)]
        kt = [f"{u}_kvt{i}" for i in range(TT)]
        kq = [f"{u}_qf{i}" for i in range(3)]
        kyly = [f"{u}_yl{i}" for i in range(3)]
        idb = cx.t["s_c_ident_bf"]

        def tile_T(dst_ap, dst_key, src_ap, src_key, nr=128, nc_=128):
            cx.transp(pstb[0:nc_, 0:nr], src_ap, idb[0:nr, 0:nr],
                      reads=(src_key, "s_c_ident_bf"), writes=(kstb,))
            cx.dve(lambda e, o=dst_ap, i=pstb[0:nc_, 0:nr]: e.tensor_copy(o, i),
                   reads=(kstb,), writes=(dst_key,))

        def load_wdw(h, buf):
            srch = cx.d["L1_attn_Wd_w"].rearrange(
                "h (cc p) l -> h p cc l", p=128)[h]
            cx.dma("pool", wdw[buf][:].rearrange("p (cc l) -> p cc l", cc=CC),
                   srch, reads=(f"{u}_wdw{buf}",), writes=(f"{u}_wdw{buf}",))

        def load_outw(h, buf):
            for li in range(3):
                r0 = h * 288 + li * 128
                nrow = LCH[li]
                cx.dma("pool", outw[buf][0:nrow, li * C:(li + 1) * C],
                       cx.d["L1_attn_out_w"][r0:r0 + nrow, :],
                       reads=(f"{u}_outw{buf}",), writes=(f"{u}_outw{buf}",))

        for li in range(3):
            for tg in range(TG):
                for q in range(CC):
                    cx.mm(ps512[0:LCH[li], :],
                          latw[:, q * 288 + li * 128:
                               q * 288 + li * 128 + LCH[li]],
                          cx.t[f"XBF{q}"][:, tg * 512:(tg + 1) * 512],
                          start=(q == 0), stop=(q == CC - 1),
                          reads=(f"{u}_latw", f"XBF{q}"), writes=(k512,))
                cx.dve(lambda e, o=kv_fm[li][:, tg * 512:(tg + 1) * 512],
                       i=ps512[0:LCH[li], :],
                       bb=cx.t["s_pm_l1a_latb"][0:LCH[li], li:li + 1]:
                       e.tensor_scalar(out=o, in0=i, scalar1=bb, scalar2=None,
                                       op0=ALU.add),
                       reads=(k512, "s_pm_l1a_latb"), writes=(kf[li],))
        for li in range(3):
            for tt in range(TT):
                tile_T(kv_tm[tt][:, li * 128:li * 128 + LCH[li]], kt[tt],
                       kv_fm[li][:, tt * 128:(tt + 1) * 128], kf[li],
                       nr=LCH[li])
        load_wdw(0, 0)
        load_outw(0, 0)
        for h in range(16):
            wb = h % 2
            for li in range(3):
                for tg in range(TG):
                    for q in range(CC):
                        cx.mm(ps512[0:LCH[li], :],
                              wdw[wb][:, q * 288 + li * 128:
                                      q * 288 + li * 128 + LCH[li]],
                              cx.t[f"XBF{q}"][:, tg * 512:(tg + 1) * 512],
                              start=(q == 0), stop=(q == CC - 1),
                              reads=(f"{u}_wdw{wb}", f"XBF{q}"), writes=(k512,))
                    cx.dve(lambda e, o=q_fm[li][:, tg * 512:(tg + 1) * 512],
                           i=ps512[0:LCH[li], :],
                           bb=cx.t["s_pm_l1a_wdb"][0:LCH[li],
                                                   h * 3 + li:h * 3 + li + 1]:
                           e.tensor_scalar(out=o, in0=i, scalar1=bb,
                                           scalar2=None, op0=ALU.add),
                           reads=(k512, "s_pm_l1a_wdb"), writes=(kq[li],))
            if h < 15:
                load_wdw(h + 1, 1 - wb)
            for qi in range(TT):
                nk = (qi + 1) * 128
                ngroups = (nk + 511) // 512
                for li in range(3):
                    for g in range(ngroups):
                        n = min(512, nk - g * 512)
                        cx.mm(pssc[:, g * 512:g * 512 + n],
                              q_fm[li][0:LCH[li], qi * 128:(qi + 1) * 128],
                              kv_fm[li][:, g * 512:g * 512 + n],
                              start=(li == 0), stop=(li == 2),
                              reads=(kq[li], kf[li]), writes=(ksc,))
                cx.dve(lambda e, o=pssc[:, qi * 128:qi * 128 + 128],
                       m=cx.t["s_c_tri_causal"][:]: e.tensor_add(o, o, m),
                       reads=(ksc, "s_c_tri_causal"), writes=(ksc,))
                cx.dve(lambda e, o=small[:, 0:1], i=pssc[:, 0:nk]:
                       e.reduce_max(o, i, axis=AX.X),
                       reads=(ksc,), writes=(ksm,))
                cx.dve(lambda e, o=small[:, 1:2], i=small[:, 0:1]:
                       e.tensor_scalar_mul(o, i, -0.125),
                       reads=(ksm,), writes=(ksm,))
                cx.act(lambda e, o=att[:, 0:nk], i=pssc[:, 0:nk],
                       bb=small[:, 1:2]:
                       e.activation(out=o, in_=i, func=AF.Exp, scale=0.125,
                                    bias=bb),
                       reads=(ksc, ksm), writes=(f"{u}_att",))
                cx.dve(lambda e, o=small[:, 2:3], i=att[:, 0:nk]:
                       e.reduce_sum(o, i, axis=AX.X),
                       reads=(f"{u}_att",), writes=(ksm,))
                cx.dve(lambda e, o=small[:, 3:4], i=small[:, 2:3]:
                       e.reciprocal(o, i),
                       reads=(ksm,), writes=(ksm,))
                cx.dve(lambda e, o=att[:, 0:nk], i=att[:, 0:nk],
                       s=small[:, 3:4]:
                       e.tensor_scalar(out=o, in0=i, scalar1=s, scalar2=None,
                                       op0=ALU.mult),
                       reads=(f"{u}_att", ksm), writes=(f"{u}_att",))
                for kti in range(qi + 1):
                    tile_T(attT[kti][:], f"{u}_attT{kti}",
                           att[:, kti * 128:(kti + 1) * 128], f"{u}_att")
                for li in range(3):
                    for kti in range(qi + 1):
                        cx.mm(psyl[0:LCH[li], :],
                              kv_tm[kti][:, li * 128:li * 128 + LCH[li]],
                              attT[kti][:],
                              start=(kti == 0), stop=(kti == qi),
                              reads=(kt[kti], f"{u}_attT{kti}"), writes=(kyl,))
                    cx.dve(lambda e,
                           o=ylat[li][0:LCH[li], qi * 128:(qi + 1) * 128],
                           i=psyl[0:LCH[li], :]: e.tensor_copy(o, i),
                           reads=(kyl,), writes=(kyly[li],))
            for ccx in range(CC):
                for tg in range(TG):
                    for li in range(3):
                        cx.mm(ps512[:],
                              outw[wb][0:LCH[li],
                                       li * C + ccx * 128:
                                       li * C + (ccx + 1) * 128],
                              ylat[li][0:LCH[li], tg * 512:(tg + 1) * 512],
                              start=(li == 0), stop=(li == 2),
                              reads=(f"{u}_outw{wb}", kyly[li]),
                              writes=(k512,))
                    cx.dve(lambda e,
                           o=cx.t[f"X{ccx}"][:, tg * 512:(tg + 1) * 512],
                           i=ps512[:]: e.tensor_add(o, o, i),
                           reads=(k512, f"X{ccx}"), writes=(f"X{ccx}",))
            if h < 15:
                load_outw(h + 1, 1 - wb)
        for ccx in range(CC):
            cx.dve(lambda e, o=cx.t[f"X{ccx}"][:],
                   bb=cx.t["s_pm_l1a_outb"][:, ccx:ccx + 1]:
                   e.tensor_scalar(out=o, in0=o, scalar1=bb, scalar2=None,
                                   op0=ALU.add),
                   reads=(f"X{ccx}", "s_pm_l1a_outb"), writes=(f"X{ccx}",))
        p.barrier()
    emit_postln(cx, f"{u}ln")


def emit_l1ffn(cx, b):
    p = cx.p
    u = f"lf{b}"
    HID = 2730
    scr = cx.d["scr_xouter"]
    for ci in range(CC):
        cx.dma("sp", scr[ci * 128:(ci + 1) * 128, :], cx.t[f"X{ci}"][:],
               reads=(f"X{ci}",), writes=(f"scrx{ci}",))
    for l in range(2):
        emit_l1ffn_layer(cx, b, l)
    for ci in range(CC):
        cx.p.op("pool",
                lambda e, o=cx.t[f"X{ci}"][:],
                i=scr[ci * 128:(ci + 1) * 128, :]:
                e.dma_start(out=o, in_=i, accum_op=ALU.add),
                reads=(f"scrx{ci}", f"X{ci}"), writes=(f"X{ci}",),
                dma=True, tag="xreload")
    p.barrier()
    emit_postln(cx, f"{u}ln")


def emit_l1ffn_layer(cx, b, l):
    p = cx.p
    u = f"lf{b}_{l}"
    HID = 2730
    s1 = cx.d["scr_l1w1"]
    s2 = cx.d["scr_l1w2"]

    def shifted_bf(ci, tg, shcb):
        if ci < 4:
            return cx.t[f"XBF{ci}"][:, tg * 512:(tg + 1) * 512], f"XBF{ci}"
        if tg == 0:
            return shcb[ci - 4][:], f"{u}_shcb{ci - 4}"
        return (cx.t[f"XBF{ci}"][:, tg * 512 - 1:tg * 512 + 511],
                f"XBF{ci}")

    with ExitStack() as es0:
        qr = [cx.sb(es0, f"{u}_qr{i}", [128, T], BF16) for i in range(4)]
        kr = [cx.sb(es0, f"{u}_kr{i}", [128, T], BF16) for i in range(4)]
        vfm = [cx.sb(es0, f"{u}_vf{i}", [128, T], BF16) for i in range(4)]
        ofm = [cx.sb(es0, f"{u}_of{i}", [128, T], BF16) for i in range(4)]
        kq = [f"{u}_qr{i}" for i in range(4)]
        kk = [f"{u}_kr{i}" for i in range(4)]
        kv = [f"{u}_vf{i}" for i in range(4)]
        ko = [f"{u}_of{i}" for i in range(4)]
        shcb = [cx.sb(es0, f"{u}_shcb{i}", [128, 512], BF16) for i in range(4)]
        ps512 = cx.ps(es0, f"{u}_ps512", [128, 512], F32)
        k512 = f"{u}_ps512"
        pstb = cx.ps(es0, f"{u}_pstb", [128, 128], BF16)
        kstb = f"{u}_pstb"
        idb = cx.t["s_c_ident_bf"]

        for i in range(4):
            kx = f"XBF{i + 4}"
            cx.dve(lambda e, o=shcb[i][:, 0:1]: e.memset(o, 0.0),
                   reads=(), writes=(f"{u}_shcb{i}",))
            cx.dve(lambda e, o=shcb[i][:, 1:512], s=cx.t[kx][:, 0:511]:
                   e.tensor_copy(o, s),
                   reads=(kx,), writes=(f"{u}_shcb{i}",))

        with ExitStack() as es1:
            wq = [cx.sb(es1, f"{u}_wq{i}", [128, CC * 128], BF16)
                  for i in range(2)]
            rot = [cx.sb(es1, f"{u}_rot{i}", [128, 512], F32) for i in range(4)]
            rmat = cx.sb(es1, f"{u}_rmat", [128, 128], F32)
            stg = [cx.sb(es1, f"{u}_st{i}", [128, 512], F32) for i in range(5)]
            rows = cx.sb(es1, f"{u}_rows", [2, 512 * 2], F32)
            psns = cx.ps(es1, f"{u}_psns", [2, 512], F32)
            psrs = cx.ps(es1, f"{u}_psrs", [128, 512], F32)
            psrh = cx.ps(es1, f"{u}_psrh", [128, 512], F32)
            kst = [f"{u}_st{i}" for i in range(5)]
            krow, kns, krs, krh = (f"{u}_rows", f"{u}_psns", f"{u}_psrs",
                                   f"{u}_psrh")
            cx.dma("sp", rmat[:], cx.d["c_rmat"], reads=(),
                   writes=(f"{u}_rmat",))
            wqkv = cx.d["L1_ffn_wqkv"]

            def load_wq(oc, buf):
                srcw = wqkv[l][:, oc * 128:(oc + 1) * 128].rearrange(
                    "(cc p) n -> p cc n", p=128)
                dst = wq[buf][:].rearrange("p (cc n) -> p cc n", cc=CC)
                cx.dma("pool", dst, srcw, reads=(f"{u}_wq{buf}",),
                       writes=(f"{u}_wq{buf}",))

            load_wq(0, 0)
            load_wq(1, 1)
            for tg in range(TG):
                for ti, tn in enumerate(("c_rot_cosq", "c_rot_sinq",
                                         "c_rot_cosk", "c_rot_sink")):
                    cx.dma("sp", rot[ti][:],
                           cx.d[tn][:, tg * 512:(tg + 1) * 512],
                           reads=(f"{u}_rot{ti}",), writes=(f"{u}_rot{ti}",))
                for oc in range(12):
                    buf = oc % 2
                    for q in range(CC):
                        rhs, rk = shifted_bf(q, tg, shcb)
                        cx.mm(ps512[:], wq[buf][:, q * 128:(q + 1) * 128],
                              rhs, start=(q == 0), stop=(q == CC - 1),
                              reads=(f"{u}_wq{buf}", rk), writes=(k512,))
                    if not (tg == TG - 1 and oc >= 10):
                        load_wq((oc + 2) % 12, buf)
                    if oc >= 8:
                        cx.dve(lambda e,
                               o=vfm[oc - 8][:, tg * 512:(tg + 1) * 512],
                               i=ps512[:]: e.tensor_copy(o, i),
                               reads=(k512,), writes=(kv[oc - 8],))
                        continue
                    is_q = oc < 4
                    dst = (qr[oc] if is_q else kr[oc - 4])
                    dk = (kq[oc] if is_q else kk[oc - 4])
                    ct = rot[0] if is_q else rot[2]
                    st_ = rot[1] if is_q else rot[3]
                    ctk = f"{u}_rot{0 if is_q else 2}"
                    stk = f"{u}_rot{1 if is_q else 3}"
                    scl = ("s_pm_l1f_qs" if is_q else "s_pm_l1f_ks")
                    cx.dve(lambda e, o=stg[0][:], i=ps512[:]:
                           e.tensor_copy(o, i),
                           reads=(k512,), writes=(kst[0],))
                    cx.dve(lambda e, o=stg[1][:], i=stg[0][:]:
                           e.tensor_mul(o, i, i),
                           reads=(kst[0],), writes=(kst[1],))
                    cx.mm(psns[:], cx.t["s_c_hsel"][:], stg[1][:],
                          start=True, stop=True,
                          reads=("s_c_hsel", kst[1]), writes=(kns,))
                    cx.dve(lambda e, o=rows[0:2, 0:512], i=psns[:]:
                           e.tensor_scalar_max(o, i, 1e-24),
                           reads=(kns,), writes=(krow,))
                    cx.act(lambda e, o=rows[0:2, 512:1024], i=rows[0:2, 0:512]:
                           e.activation(out=o, in_=i, func=AF.Ln),
                           reads=(krow,), writes=(krow,))
                    cx.act(lambda e, o=rows[0:2, 512:1024],
                           i=rows[0:2, 512:1024]:
                           e.activation(out=o, in_=i, func=AF.Exp, scale=-0.5),
                           reads=(krow,), writes=(krow,))
                    cx.dve(lambda e, o=rows[0:2, 0:512], i=rows[0:2, 512:1024]:
                           e.tensor_copy(o, i),
                           reads=(krow,), writes=(krow,))
                    cx.mm(psrs[:], cx.t["s_c_hselT"][0:2, :], rows[0:2, 0:512],
                          start=True, stop=True,
                          reads=("s_c_hselT", krow), writes=(krs,))
                    cx.dve(lambda e, o=stg[2][:], i=stg[0][:],
                           s=cx.t[scl][:, l:l + 1], rs=psrs[:]:
                           e.scalar_tensor_tensor(out=o, in0=i, scalar=s,
                                                  in1=rs, op0=ALU.mult,
                                                  op1=ALU.mult),
                           reads=(kst[0], scl, krs), writes=(kst[2],))
                    cx.mm(psrh[:], rmat[:], stg[2][:], start=True, stop=True,
                          reads=(f"{u}_rmat", kst[2]), writes=(krh,))
                    cx.dve(lambda e, o=stg[3][:], a=stg[2][:], c=ct[:]:
                           e.tensor_mul(o, a, c),
                           reads=(kst[2], ctk), writes=(kst[3],))
                    cx.dve(lambda e, o=stg[4][:], a=psrh[:], c=st_[:]:
                           e.tensor_mul(o, a, c),
                           reads=(krh, stk), writes=(kst[4],))
                    cx.dve(lambda e, o=dst[:, tg * 512:(tg + 1) * 512],
                           a=stg[3][:], c=stg[4][:]: e.tensor_add(o, a, c),
                           reads=(kst[3], kst[4]), writes=(dk,))
            p.barrier()

        with ExitStack() as es2:
            vtw = [cx.sb(es2, f"{u}_vtw{i}", [128, 8 * 128], BF16)
                   for i in range(4)]
            kvt = [f"{u}_vtw{i}" for i in range(4)]
            attl = cx.sb(es2, f"{u}_attl", [128, 1024], BF16)
            attT = [cx.sb(es2, f"{u}_attT{i}", [128, 128], BF16)
                    for i in range(8)]
            small = cx.sb(es2, f"{u}_small", [128, 4], F32)
            tria = cx.sb(es2, f"{u}_tria", [128, 128], F32)
            cx.dma("sp", tria[:], cx.d["c_tri_anti"], reads=(),
                   writes=(f"{u}_tria",))
            psl = cx.ps(es2, f"{u}_psl", [128, 1024], F32)
            psy = cx.ps(es2, f"{u}_psy", [64, 128], F32)
            kl, ky2, ksm = f"{u}_psl", f"{u}_psy", f"{u}_small"
            katt = f"{u}_attl"

            for w in range(4):
                for vj in range(4):
                    for kt2 in range(8):
                        gt = 4 * (w - 1) + kt2
                        if gt < 0 or gt >= TT:
                            continue
                        cx.transp(pstb[:],
                                  vfm[vj][:, gt * 128:(gt + 1) * 128], idb[:],
                                  reads=(kv[vj], "s_c_ident_bf"),
                                  writes=(kstb,))
                        cx.dve(lambda e,
                               o=vtw[vj][:, kt2 * 128:(kt2 + 1) * 128],
                               i=pstb[:]: e.tensor_copy(o, i),
                               reads=(kstb,), writes=(kvt[vj],))
                for h in range(8):
                    hc, hb = h // 2, (h % 2) * 64
                    for qi4 in range(4):
                        qi = w * 4 + qi4
                        o_ = qi4 * 128
                        if w > 0:
                            cx.mm(psl[:, 0:512],
                                  qr[hc][hb:hb + 64, qi * 128:(qi + 1) * 128],
                                  kr[hc][hb:hb + 64,
                                         (w - 1) * 512:w * 512],
                                  start=True, stop=True,
                                  reads=(kq[hc], kk[hc]), writes=(kl,))
                        cx.mm(psl[:, 512:1024],
                              qr[hc][hb:hb + 64, qi * 128:(qi + 1) * 128],
                              kr[hc][hb:hb + 64, w * 512:(w + 1) * 512],
                              start=True, stop=True,
                              reads=(kq[hc], kk[hc]), writes=(kl,))
                        if w == 0:
                            cx.dve(lambda e, o=psl[:, 0:512]:
                                   e.memset(o, NEG),
                                   reads=(), writes=(kl,))
                        else:
                            if o_ > 0:
                                cx.dve(lambda e, o=psl[:, 0:o_]:
                                       e.memset(o, NEG),
                                       reads=(), writes=(kl,))
                            cx.dve(lambda e,
                                   o=psl[:, o_:o_ + 128],
                                   m=tria[:]: e.tensor_add(o, o, m),
                                   reads=(f"{u}_tria", kl), writes=(kl,))
                        cx.dve(lambda e, o=psl[:, 512 + o_:512 + o_ + 128],
                               m=cx.t["s_c_tri_causal"][:]:
                               e.tensor_add(o, o, m),
                               reads=("s_c_tri_causal", kl), writes=(kl,))
                        if o_ < 384:
                            cx.dve(lambda e, o=psl[:, 512 + o_ + 128:1024]:
                                   e.memset(o, NEG),
                                   reads=(), writes=(kl,))
                        cx.dve(lambda e, o=small[:, 0:1], i=psl[:]:
                               e.reduce_max(o, i, axis=AX.X),
                               reads=(kl,), writes=(ksm,))
                        cx.dve(lambda e, o=small[:, 1:2], i=small[:, 0:1]:
                               e.tensor_scalar_mul(o, i, -8.0),
                               reads=(ksm,), writes=(ksm,))
                        cx.act(lambda e, o=attl[:], i=psl[:],
                               bb=small[:, 1:2]:
                               e.activation(out=o, in_=i, func=AF.Exp,
                                            scale=8.0, bias=bb),
                               reads=(kl, ksm), writes=(katt,))
                        cx.dve(lambda e, o=small[:, 2:3], i=attl[:]:
                               e.reduce_sum(o, i, axis=AX.X),
                               reads=(katt,), writes=(ksm,))
                        cx.dve(lambda e, o=small[:, 3:4], i=small[:, 2:3]:
                               e.reciprocal(o, i),
                               reads=(ksm,), writes=(ksm,))
                        cx.dve(lambda e, o=attl[:], i=attl[:],
                               s=small[:, 3:4]:
                               e.tensor_scalar(out=o, in0=i, scalar1=s,
                                               scalar2=None, op0=ALU.mult),
                               reads=(katt, ksm), writes=(katt,))
                        kts = [kt2 for kt2 in range(qi4, qi4 + 5)
                               if 0 <= 4 * (w - 1) + kt2 < TT]
                        for kt2 in kts:
                            cx.transp(pstb[:],
                                      attl[:, kt2 * 128:(kt2 + 1) * 128],
                                      idb[:],
                                      reads=(katt, "s_c_ident_bf"),
                                      writes=(kstb,))
                            cx.dve(lambda e, o=attT[kt2][:], i=pstb[:]:
                                   e.tensor_copy(o, i),
                                   reads=(kstb,), writes=(f"{u}_attT{kt2}",))
                        for ji, kt2 in enumerate(kts):
                            cx.mm(psy[:],
                                  vtw[hc][:, kt2 * 128 + hb:
                                          kt2 * 128 + hb + 64],
                                  attT[kt2][:],
                                  start=(ji == 0), stop=(ji == len(kts) - 1),
                                  reads=(kvt[hc], f"{u}_attT{kt2}"),
                                  writes=(ky2,))
                        cx.dve(lambda e,
                               o=ofm[hc][hb:hb + 64,
                                         qi * 128:(qi + 1) * 128],
                               i=psy[0:64, :]: e.tensor_copy(o, i),
                               reads=(ky2,), writes=(ko[hc],))
            p.barrier()

        with ExitStack() as es3:
            wo = cx.sb(es3, f"{u}_wo", [128, 4 * C], BF16)
            srcw = cx.d["L1_ffn_wo"][l].rearrange("(ic p) c -> p ic c", p=128)
            cx.dma("pool", wo[:].rearrange("p (ic c) -> p ic c", ic=4), srcw,
                   reads=(), writes=(f"{u}_wo",))
            for ccx in range(CC):
                for tg in range(TG):
                    for ic in range(4):
                        cx.mm(ps512[:],
                              wo[:, ic * C + ccx * 128:
                                 ic * C + (ccx + 1) * 128],
                              ofm[ic][:, tg * 512:(tg + 1) * 512],
                              start=(ic == 0), stop=(ic == 3),
                              reads=(f"{u}_wo", ko[ic]), writes=(k512,))
                    cx.dve(lambda e,
                           o=cx.t[f"X{ccx}"][:, tg * 512:(tg + 1) * 512],
                           i=ps512[:]: e.tensor_add(o, o, i),
                           reads=(k512, f"X{ccx}"), writes=(f"X{ccx}",))
            p.barrier()

    with ExitStack() as es4:
        hbf = [cx.sb(es4, f"{u}_hb{i}", [128, T], BF16) for i in range(CC)]
        kh = [f"{u}_hb{i}" for i in range(CC)]
        shc = [cx.sb(es4, f"{u}_shc{i}", [128, 512], F32) for i in range(4)]
        sq = cx.sb(es4, f"{u}_sq", [128, 512], F32)
        t1 = cx.sb(es4, f"{u}_t1", [128, 512], F32)
        t2 = cx.sb(es4, f"{u}_t2", [128, 512], F32)
        rows = cx.sb(es4, f"{u}_lnrows", [1, 512 * 4], F32)
        ones1 = cx.sb(es4, f"{u}_o1", [1, 128], F32)
        cx.dma("sp", ones1[:], cx.d["c_ones1"], reads=(), writes=(f"{u}_o1",))
        es4a = ExitStack()
        psx = cx.ps(es4a, f"{u}_psx", [1, 512], F32)
        psq = cx.ps(es4a, f"{u}_psq", [1, 512], F32)
        psa = cx.ps(es4a, f"{u}_psa", [128, 512], F32)
        psb = cx.ps(es4a, f"{u}_psb", [128, 512], F32)
        kx, kq2, ka, kb = (f"{u}_psx", f"{u}_psq", f"{u}_psa", f"{u}_psb")
        kr2 = f"{u}_lnrows"
        for i in range(4):
            kxs = f"X{i + 4}"
            cx.dve(lambda e, o=shc[i][:, 0:1]: e.memset(o, 0.0),
                   reads=(), writes=(f"{u}_shc{i}",))
            cx.dve(lambda e, o=shc[i][:, 1:512], s=cx.t[kxs][:, 0:511]:
                   e.tensor_copy(o, s),
                   reads=(kxs,), writes=(f"{u}_shc{i}",))

        def shifted_f32(ci, tg):
            if ci < 4:
                return cx.t[f"X{ci}"][:, tg * 512:(tg + 1) * 512], f"X{ci}"
            if tg == 0:
                return shc[ci - 4][:], f"{u}_shc{ci - 4}"
            return (cx.t[f"X{ci}"][:, tg * 512 - 1:tg * 512 + 511], f"X{ci}")

        for tg in range(TG):
            for ci in range(CC):
                hsl, hk = shifted_f32(ci, tg)
                cx.mm(psx[:], cx.t["s_c_ones_col"][:, 0:1], hsl,
                      start=(ci == 0), stop=(ci == CC - 1),
                      reads=(hk, "s_c_ones_col"), writes=(kx,))
                cx.dve(lambda e, o=sq[:], i=hsl: e.tensor_mul(o, i, i),
                       reads=(hk,), writes=(f"{u}_sq",))
                cx.mm(psq[:], cx.t["s_c_ones_col"][:, 0:1], sq[:],
                      start=(ci == 0), stop=(ci == CC - 1),
                      reads=(f"{u}_sq", "s_c_ones_col"), writes=(kq2,))
            mean = rows[0:1, 0:512]
            var = rows[0:1, 512:1024]
            rstd = rows[0:1, 1024:1536]
            brow = rows[0:1, 1536:2048]
            cx.dve(lambda e, o=mean, i=psx[:]:
                   e.tensor_scalar_mul(o, i, 1.0 / C),
                   reads=(kx,), writes=(kr2,))
            cx.dve(lambda e, o=var, a=mean:
                   e.scalar_tensor_tensor(out=o, in0=a, scalar=-1.0, in1=a,
                                          op0=ALU.mult, op1=ALU.mult),
                   reads=(kr2,), writes=(kr2,))
            cx.dve(lambda e, o=var, a=psq[:], b2=var:
                   e.scalar_tensor_tensor(out=o, in0=a, scalar=1.0 / C, in1=b2,
                                          op0=ALU.mult, op1=ALU.add),
                   reads=(kq2, kr2), writes=(kr2,))
            cx.act(lambda e, o=rstd, i=var, ep=cx.t["s_c_eps"][0:1, 0:1]:
                   e.activation(out=o, in_=i, func=AF.Ln, bias=ep),
                   reads=(kr2, "s_c_eps"), writes=(kr2,))
            cx.act(lambda e, o=rstd, i=rstd:
                   e.activation(out=o, in_=i, func=AF.Exp, scale=-0.5),
                   reads=(kr2,), writes=(kr2,))
            cx.dve(lambda e, o=brow, a=mean, b2=rstd:
                   e.scalar_tensor_tensor(out=o, in0=a, scalar=-1.0, in1=b2,
                                          op0=ALU.mult, op1=ALU.mult),
                   reads=(kr2,), writes=(kr2,))
            cx.mm(psa[:], ones1[0:1, :], rstd, start=True, stop=True,
                  reads=(kr2, f"{u}_o1"), writes=(ka,))
            cx.mm(psb[:], ones1[0:1, :], brow, start=True, stop=True,
                  reads=(kr2, f"{u}_o1"), writes=(kb,))
            for ci in range(CC):
                hsl, hk = shifted_f32(ci, tg)
                cx.dve(lambda e, o=t1[:], a=hsl, bb=psa[:]:
                       e.tensor_mul(o, a, bb),
                       reads=(hk, ka), writes=(f"{u}_t1",))
                cx.dve(lambda e, o=t2[:], a=t1[:], bb=psb[:]:
                       e.tensor_add(o, a, bb),
                       reads=(f"{u}_t1", kb), writes=(f"{u}_t2",))
                cx.dve(lambda e, o=hbf[ci][:, tg * 512:(tg + 1) * 512],
                       a=t2[:],
                       g=cx.t["s_pm_l1f_lng"][:, l * 8 + ci:l * 8 + ci + 1],
                       bt=cx.t["s_pm_l1f_lnb"][:, l * 8 + ci:l * 8 + ci + 1]:
                       e.tensor_scalar(out=o, in0=a, scalar1=g, scalar2=bt,
                                       op0=ALU.mult, op1=ALU.add),
                       reads=(f"{u}_t2", "s_pm_l1f_lng", "s_pm_l1f_lnb"),
                       writes=(kh[ci],))
        p.barrier()
        es4a.close()

        with ExitStack() as es5:
            w1a = [cx.sb(es5, f"{u}_w1a{i}", [128, CC * 128], BF16)
                   for i in range(2)]
            w1g = [cx.sb(es5, f"{u}_w1g{i}", [128, CC * 128], BF16)
                   for i in range(2)]
            w2s = [cx.sb(es5, f"{u}_w2s{i}", [128, 22 * 128], BF16)
                   for i in range(2)]
            ges = cx.sb(es5, f"{u}_ges", [128, 512], BF16)
            h2 = [cx.sb(es5, f"{u}_h2_{i}", [128, 512], BF16)
                  for i in range(22)]
            psag = [cx.ps(es5, f"{u}_psag{i}", [128, 512], F32)
                    for i in range(4)]
            pso = [cx.ps(es5, f"{u}_pso{i}", [128, 512], F32) for i in range(2)]
            kag = [f"{u}_psag{i}" for i in range(4)]
            kpo = [f"{u}_pso{i}" for i in range(2)]

            def load_w1(fi, buf):
                cx.dma("sp", w1a[buf][:], s1[l, fi],
                       reads=(f"{u}_w1a{buf}",), writes=(f"{u}_w1a{buf}",))
                cx.dma("sp", w1g[buf][:], s1[l, 22 + fi],
                       reads=(f"{u}_w1g{buf}",), writes=(f"{u}_w1g{buf}",))

            def load_w2(ccx, buf):
                cx.dma("sp", w2s[buf][:], s2[l, ccx],
                       reads=(f"{u}_w2s{buf}",), writes=(f"{u}_w2s{buf}",))

            for tg in range(TG):
                load_w1(0, 0)
                load_w1(1, 1)
                for fi in range(22):
                    nf = 128 if fi < 21 else 42
                    buf = fi % 2
                    pa, pg = psag[2 * buf], psag[2 * buf + 1]
                    for q in range(CC):
                        cx.mm(pa[0:nf, :],
                              w1a[buf][:, q * 128:q * 128 + nf],
                              hbf[q][:, tg * 512:(tg + 1) * 512],
                              start=(q == 0), stop=(q == CC - 1),
                              reads=(f"{u}_w1a{buf}", kh[q]),
                              writes=(kag[2 * buf],))
                    for q in range(CC):
                        cx.mm(pg[0:nf, :],
                              w1g[buf][:, q * 128:q * 128 + nf],
                              hbf[q][:, tg * 512:(tg + 1) * 512],
                              start=(q == 0), stop=(q == CC - 1),
                              reads=(f"{u}_w1g{buf}", kh[q]),
                              writes=(kag[2 * buf + 1],))
                    if fi < 20:
                        load_w1(fi + 2, buf)
                    cx.act(lambda e, o=ges[0:nf, :], i=pg[0:nf, :],
                           bb=cx.t["s_pm_l1f_b1g"][0:nf,
                                                   l * 22 + fi:l * 22 + fi + 1]:
                           e.activation(out=o, in_=i, func=AF.Gelu, bias=bb),
                           reads=(kag[2 * buf + 1], "s_pm_l1f_b1g"),
                           writes=(f"{u}_ges",))
                    cx.dve(lambda e, o=h2[fi][0:nf, :], i=pa[0:nf, :],
                           bb=cx.t["s_pm_l1f_b1a"][0:nf,
                                                   l * 22 + fi:l * 22 + fi + 1],
                           gg=ges[0:nf, :]:
                           e.scalar_tensor_tensor(out=o, in0=i, scalar=bb,
                                                  in1=gg, op0=ALU.add,
                                                  op1=ALU.mult),
                           reads=(kag[2 * buf], f"{u}_ges"),
                           writes=(f"{u}_h2_{fi}",))
                load_w2(0, 0)
                load_w2(1, 1)
                for ccx in range(CC):
                    buf = ccx % 2
                    for fi in range(22):
                        nf = 128 if fi < 21 else 42
                        cx.mm(pso[buf][:],
                              w2s[buf][0:nf, fi * 128:(fi + 1) * 128],
                              h2[fi][0:nf, :],
                              start=(fi == 0), stop=(fi == 21),
                              reads=(f"{u}_w2s{buf}", f"{u}_h2_{fi}"),
                              writes=(kpo[buf],))
                    if ccx < 6:
                        load_w2(ccx + 2, buf)
                    cx.dve(lambda e,
                           o=cx.t[f"X{ccx}"][:, tg * 512:(tg + 1) * 512],
                           i=pso[buf][:],
                           bb=cx.t["s_pm_l1f_b2"][:,
                                                  l * 8 + ccx:l * 8 + ccx + 1]:
                           e.scalar_tensor_tensor(out=o, in0=i, scalar=bb,
                                                  in1=o, op0=ALU.add,
                                                  op1=ALU.add),
                           reads=(kpo[buf], "s_pm_l1f_b2", f"X{ccx}"),
                           writes=(f"X{ccx}",))
            p.barrier()



def _sem_hygiene_range(nc):
    kr = nc._kernel_sem_range
    start = kr.start
    n_special = 2
    if nc._bir_kernel_barrier_sem is not None:
        n_special += 1
    n_special += len(nc._monotonic_sems)
    return range(start + n_special, kr.stop)


def build_program(blocks=ALL_BLOCKS):
    nc = bass.Bass("TRN2", target_bir_lowering=False,
                   dynamic_dma_scratch_size=16384)
    rng = _sem_hygiene_range(nc)
    nc.gpsimd.dma_reset(rng)
    nc.gpsimd.sem_clear(rng)
    nc.all_engine_barrier()
    prog = Prog(nc)
    cx = Ctx(nc, prog)

    consts = host_constants()
    cx.dram("x_in", [B_CORE, T, C], F32, "ExternalInput")
    cx.dram("out", [B_CORE, T, C], F32, "ExternalOutput")
    consts_meta = {}
    for name, arr in consts.items():
        dt = BF16 if arr.dtype == BF else F32
        cx.dram(name, arr.shape, dt, "ExternalInput")
        consts_meta[name] = (list(arr.shape), dt)
    for name, shape in PM_SHAPES.items():
        cx.dram(name, shape, F32, "ExternalInput")
        consts_meta[name] = (shape, F32)
    for name, shape in W_SHAPES.items():
        cx.dram(name, shape, F32, "ExternalInput")
    cx.dram("scr_xouter", [C, T], F32, "Internal")
    cx.dram("scr_w1bf", [4, 128, CC * 4096], BF16, "Internal")
    cx.dram("scr_w2bf", [4, 8, 128, 4 * 1024], BF16, "Internal")
    cx.dram("scr_l1w1", [2, 44, 128, CC * 128], BF16, "Internal")
    cx.dram("scr_l1w2", [2, 8, 128, 22 * 128], BF16, "Internal")
    cx.consts_meta = consts_meta

    with ExitStack() as es:
        for ci in range(CC):
            cx.sb(es, f"X{ci}", [128, T], F32)
        for ci in range(CC):
            cx.sb(es, f"XBF{ci}", [128, T], BF16)
        cx.load_consts(es, GLOBAL_CONSTS + list(PM_SHAPES.keys()), consts_meta)
        prog.barrier()
        if "l0ffn" in blocks and not _dbg("nopcv"):
            emit_preconvert_moe(cx)
        if "l1ffn" in blocks:
            emit_preconvert_l1(cx)

        for b in range(B_CORE):
            emit_load_x(cx, b)
            for blk in blocks:
                emit_block(cx, blk, b)
            emit_store_x(cx, b)
        prog.barrier()

        prog.finalize()
        with ExitStack() as sem_es:
            sems = {}
            for e in ENGINES:
                sems[("c", e)] = sem_es.enter_context(nc.semaphore(f"c_{e}"))
            for (e, s) in sorted(prog.dma_uses.keys()):
                sems[("d", e, s)] = sem_es.enter_context(
                    nc.semaphore(f"d_{e}_{s}"))
            with nc.Block() as block:
                prog.emit(block, sems)
        nc.all_engine_barrier()
        nc.gpsimd.dma_reset(rng)
        nc.gpsimd.sem_clear(rng)
        nc.all_engine_barrier()
    return nc



def make_in_maps(inputs, blocks=ALL_BLOCKS):
    consts = host_constants()
    staged = host_stage(inputs)
    x = np.ascontiguousarray(np.asarray(inputs["L0_attn_x"], np.float32))
    in_maps = []
    for c in range(N_CORE):
        m = {"x_in": x[c * B_CORE:(c + 1) * B_CORE]}
        m.update(consts)
        m.update(staged)
        for k in PASS_THROUGH:
            m[k] = np.ascontiguousarray(np.asarray(inputs[k], np.float32))
        in_maps.append(m)
    return in_maps


def kernel(**inputs):
    nc = build_program(ALL_BLOCKS)
    in_maps = make_in_maps(inputs)
    res = run_bass_kernel_spmd(nc, in_maps, core_ids=list(range(N_CORE)))
    return np.concatenate([r["out"] for r in res.results], axis=0)
